# version 05
# baseline (speedup 1.0000x reference)
import math
import numpy as np
import concourse.bass as bass
import concourse.mybir as mybir
from concourse.bass_utils import run_bass_kernel_spmd

F32 = mybir.dt.float32
BF16 = mybir.dt.bfloat16
AF = mybir.ActivationFunctionType
ALU = mybir.AluOpType
AX = mybir.AxisListType

D = 4096
NC = 32
T = 1024
S = 4096
NCORES = 8
EPS = 1e-5
SHIFT = -24.0
INV_SQRT_D = float(np.float32(1.0) / np.sqrt(np.float32(128.0)))
QKVF = 3 * D
NOC_QKV = QKVF // 512


class P:
    def __init__(self, nc, blk, es):
        self.nc, self.blk = nc, blk
        self.sem = {}
        self.cnt = {}
        for s in ("ld", "ldg", "st", "t", "v", "s"):
            self.sem[s] = es.enter_context(nc.semaphore(s))
            self.cnt[s] = 0
        self.buf = {}

    def inc(self, name, amount):
        self.cnt[name] += amount
        return self.sem[name], amount

    def at(self, name):
        return self.sem[name], self.cnt[name]


def _alloc_common(p, es, with_qkv):
    nc = p.nc
    bf = p.buf
    sb = lambda n, shp, dt: es.enter_context(nc.sbuf_tensor(n, shp, dt))
    ps = lambda n, shp, dt: es.enter_context(nc.psum_tensor(n, shp, dt))
    bf["h_sb"] = sb("h_sb", [128, NC * T], BF16)
    bf["xst"] = sb("xst", [128, T], F32)
    bf["xstat"] = sb("xstat", [128, 2 * T], BF16)
    bf["gw_sb"] = sb("gw_sb", [128, NC, 8], BF16)
    bf["A_sb"] = sb("A_sb", [128, NC, 128], BF16)
    bf["Bc_sb"] = sb("Bc_sb", [128, NC, 128], BF16)
    if with_qkv:
        bf["wq_sb"] = sb("wq_sb", [128, NC, 512], BF16)
        bf["bw_dst"] = bf["wq_sb"][:][:, :, 0:128]
        bf["bw_fc"] = lambda fc: bf["wq_sb"][:][:, fc, 0:128]
    else:
        bw = sb("bw_sb", [128, NC, 128], BF16)
        bf["bw_dst"] = bw[:]
        bf["bw_fc"] = lambda fc: bw[:][:, fc, :]
    bf["bias_bf"] = sb("bias_bf", [1, D], BF16)
    bf["ones512_bf"] = sb("ones512_bf", [1, 512], BF16)
    bf["id_bf"] = sb("id_bf", [128, 128], BF16)
    bf["id_f"] = sb("id_f", [128, 128], F32)
    bf["E_sb"] = sb("E_sb", [8, 128], F32)
    bf["ones128"] = sb("ones128", [128, 1], BF16)
    bf["ones_row_f"] = sb("ones_row_f", [1, 128], F32)
    bf["ones_row_bf"] = sb("ones_row_bf", [1, 128], BF16)
    for n in ("mrow", "msqrow", "rsrow", "tmprow"):
        bf[n] = sb(n, [1, T], F32)
    bf["lgsb"] = sb("lgsb", [8, T], BF16)
    bf["lgt"] = sb("lgt", [128, 64], F32)
    for n in ("m1", "m2", "d12", "w1", "w2"):
        bf[n] = sb(n, [128, 8], F32)
    for n in ("eq1", "eq2", "msk", "wf1", "wf2", "wff"):
        bf[n] = sb(n, [128, 64], F32)
    bf["wt_sb"] = sb("wt_sb", [8, T], F32)
    bf["wexp_sb"] = sb("wexp_sb", [128, T], F32)
    bf["lw_bf"] = sb("lw_bf", [128, T], BF16)
    bf["tsb"] = sb("tsb", [128, T], F32)
    bf["xr"] = sb("xr", [128, T], F32)
    bf["x1sb"] = sb("x1sb", [128, T], F32)
    if with_qkv:
        bf["qb_row"] = sb("qb_row", [1, 512], BF16)
        bf["cos_sb"] = sb("cos_sb", [128, 8, 128], F32)
        bf["sin_sb"] = sb("sin_sb", [128, 8, 128], F32)
        bf["qsb"] = sb("qsb", [128, 512], F32)
        bf["rop1"] = sb("rop1", [128, 512], F32)
        bf["rop2"] = sb("rop2", [128, 512], F32)
        bf["ropo"] = sb("ropo", [128, 512], F32)
    bf["bk"] = [ps("bk%d" % i, [128, 512], F32) for i in range(7)]
    bf["ptb"] = ps("ptb", [128, 64], BF16)


def emit_consts(p, c, with_qkv, with_obias):
    blk, bf = p.blk, p.buf

    @blk.gpsimd
    def _(g):
        g.dma_start(out=bf["gw_sb"][:], in_=c["gwT"].rearrange("(c p) e -> p c e", p=128)).then_inc(*p.inc("ldg", 16))
        g.dma_start(out=bf["A_sb"][:], in_=c["AT"].rearrange("(c p) r -> p c r", p=128)).then_inc(*p.inc("ldg", 16))
        g.dma_start(out=bf["Bc_sb"][:], in_=c["BcT"].rearrange("e (c p) -> e c p", p=128)).then_inc(*p.inc("ldg", 16))
        g.dma_start(out=bf["id_bf"][:], in_=c["idm"]).then_inc(*p.inc("ldg", 16))
        g.dma_start(out=bf["bias_bf"][:], in_=c["bb"][:]).then_inc(*p.inc("ldg", 16))
        if with_obias:
            g.dma_start(out=p.buf["obias_bf"][:], in_=c["ob"][:]).then_inc(*p.inc("ldg", 16))
    p.m_ldg_consts = p.cnt["ldg"]

    @blk.sync
    def _(s):
        s.dma_start(out=bf["id_f"][:], in_=c["idm"]).then_inc(*p.inc("ld", 16))
        s.dma_start(out=bf["E_sb"][:], in_=c["Emat"]).then_inc(*p.inc("ld", 16))
        if with_qkv:
            s.dma_start(out=bf["cos_sb"][:], in_=c["cosT"].rearrange("(n p) j -> p n j", p=128)).then_inc(*p.inc("ld", 16))
            s.dma_start(out=bf["sin_sb"][:], in_=c["sinT"].rearrange("(n p) j -> p n j", p=128)).then_inc(*p.inc("ld", 16))
    m_ld = p.cnt["ld"]

    @blk.vector
    def _(v, m_ld=m_ld):
        v.memset(bf["ones128"][:], 1.0)
        v.memset(bf["ones_row_f"][:], 1.0)
        v.memset(bf["ones_row_bf"][:], 1.0).then_inc(*p.inc("v", 1))
        v.wait_ge(p.sem["ld"], m_ld)
        v.memset(bf["ones512_bf"][:], 1.0).then_inc(*p.inc("v", 1))
    p.m_v_consts = p.cnt["v"]
    p.m_ld_consts = m_ld


def emit_ln_stats_stream(p, src3, pm, psq, first_ld_wait=None):
    blk, bf = p.blk, p.buf
    prev_s = {}
    prev_t = {}
    for fc in range(NC):
        @blk.sync
        def _(s, fc=fc):
            if fc == 0 and first_ld_wait is not None:
                for nm, mk in first_ld_wait:
                    s.wait_ge(p.sem[nm], mk)
            if fc > 0:
                s.wait_ge(p.sem["s"], prev_s[fc - 1])
            s.dma_start(out=bf["xst"][:], in_=src3[:, fc, :]).then_inc(*p.inc("ld", 16))
        m_ld = p.cnt["ld"]

        @blk.scalar
        def _(sc, fc=fc, m_ld=m_ld):
            sc.wait_ge(p.sem["ld"], m_ld)
            if fc > 0:
                sc.wait_ge(p.sem["t"], prev_t[fc - 1])
            sc.copy(bf["xstat"][:, 0:T], bf["xst"][:])
            sc.activation(bf["xstat"][:, T:2 * T], bf["xst"][:], AF.Square).then_inc(*p.inc("s", 1))
        prev_s[fc] = p.cnt["s"]

        @blk.tensor
        def _(t, fc=fc, m_s=prev_s[fc]):
            t.wait_ge(p.sem["s"], m_s)
            if fc == 0:
                t.wait_ge(p.sem["v"], p.m_v_consts)
            for h in range(2):
                _mm = t.matmul(pm[h][0:1, :], bf["ones128"][:], bf["xstat"][:, h * 512:(h + 1) * 512],
                         start=(fc == 0), stop=(fc == NC - 1))
                _mm = t.matmul(psq[h][0:1, :], bf["ones128"][:], bf["xstat"][:, T + h * 512:T + (h + 1) * 512],
                         start=(fc == 0), stop=(fc == NC - 1))
            _mm.then_inc(*p.inc("t", 1))
        prev_t[fc] = p.cnt["t"]
    return prev_t[NC - 1]


def emit_ln_scales(p, pm, psq, prs, pc, m_stats_t):
    blk, bf = p.blk, p.buf
    mrow, msqrow, rsrow, tmprow = bf["mrow"], bf["msqrow"], bf["rsrow"], bf["tmprow"]

    @blk.scalar
    def _(sc, m=m_stats_t):
        sc.wait_ge(p.sem["t"], m)
        sc.copy(mrow[:, 0:512], pm[0][0:1, :])
        sc.copy(mrow[:, 512:1024], pm[1][0:1, :])
        sc.copy(msqrow[:, 0:512], psq[0][0:1, :])
        sc.copy(msqrow[:, 512:1024], psq[1][0:1, :]).then_inc(*p.inc("s", 1))
    m_cp = p.cnt["s"]

    @blk.vector
    def _(v, m=m_cp):
        v.wait_ge(p.sem["s"], m)
        v.tensor_scalar(mrow[:], mrow[:], 1.0 / D, 0.0, ALU.mult, ALU.add).then_inc(*p.inc("v", 1))
        v.wait_ge(*p.at("v"))
        v.tensor_scalar(msqrow[:], msqrow[:], 1.0 / D, EPS, ALU.mult, ALU.add)
        v.tensor_mul(tmprow[:], mrow[:], mrow[:]).then_inc(*p.inc("v", 1))
        v.wait_ge(*p.at("v"))
        v.tensor_sub(msqrow[:], msqrow[:], tmprow[:]).then_inc(*p.inc("v", 1))
    m_var = p.cnt["v"]

    @blk.scalar
    def _(sc, m=m_var):
        sc.wait_ge(p.sem["v"], m)
        sc.activation(tmprow[:], msqrow[:], AF.Ln).then_inc(*p.inc("s", 1))
        sc.wait_ge(*p.at("s"))
        sc.activation(rsrow[:], tmprow[:], AF.Exp, scale=-0.5).then_inc(*p.inc("s", 1))
    m_rs = p.cnt["s"]

    @blk.vector
    def _(v, m=m_rs):
        v.wait_ge(p.sem["s"], m)
        v.scalar_tensor_tensor(tmprow[:], mrow[:], -1.0, rsrow[:], ALU.mult, ALU.mult).then_inc(*p.inc("v", 1))
    m_crow = p.cnt["v"]

    @blk.tensor
    def _(t, m=m_crow):
        t.wait_ge(p.sem["v"], m)
        for h in range(2):
            _mm = t.matmul(prs[h][:], bf["ones_row_f"][:], rsrow[0:1, h * 512:(h + 1) * 512], start=True, stop=True)
            _mm = t.matmul(pc[h][:], bf["ones_row_f"][:], tmprow[0:1, h * 512:(h + 1) * 512], start=True, stop=True)
        _mm.then_inc(*p.inc("t", 1))
    return p.cnt["t"]


def emit_ln_norm(p, src3, prs, pc, m_bcast, extra_first_waits=()):
    blk, bf = p.blk, p.buf
    prev_v = {}
    for fc in range(NC):
        @blk.sync
        def _(s, fc=fc):
            if fc == 0:
                for nm, mk in extra_first_waits:
                    s.wait_ge(p.sem[nm], mk)
            if fc > 0:
                s.wait_ge(p.sem["v"], prev_v[fc - 1])
            s.dma_start(out=bf["xst"][:], in_=src3[:, fc, :]).then_inc(*p.inc("ld", 16))
        m_ld = p.cnt["ld"]

        @blk.vector
        def _(v, fc=fc, m_ld=m_ld, m_bcast=m_bcast):
            v.wait_ge(p.sem["ld"], m_ld)
            if fc == 0:
                v.wait_ge(p.sem["t"], m_bcast)
            v.tensor_tensor(bf["tsb"][:, 0:512], bf["xst"][:, 0:512], prs[0][:], ALU.mult)
            v.tensor_tensor(bf["tsb"][:, 512:1024], bf["xst"][:, 512:1024], prs[1][:], ALU.mult).then_inc(*p.inc("v", 1))
            v.wait_ge(*p.at("v"))
            v.tensor_tensor(bf["h_sb"][:, fc * T: fc * T + 512], bf["tsb"][:, 0:512], pc[0][:], ALU.add)
            v.tensor_tensor(bf["h_sb"][:, fc * T + 512: fc * T + 1024], bf["tsb"][:, 512:1024], pc[1][:], ALU.add).then_inc(*p.inc("v", 1))
        prev_v[fc] = p.cnt["v"]
    return prev_v[NC - 1]


def emit_ffn(p, c, resid3, out3, m_h, fold):
    blk, bf = p.blk, p.buf
    bk = bf["bk"]
    pg = [bk[0], bk[1]]
    plow = [bk[2], bk[3]]
    pt2 = [bk[4], bk[5]]
    pwexp = [bk[6], bk[4]]
    pb = [bk[0], bk[1]]

    @blk.tensor
    def _(t, m_h=m_h):
        t.wait_ge(p.sem["v"], m_h)
        t.wait_ge(p.sem["ldg"], p.m_ldg_consts)
        for fc in range(NC):
            for h in range(2):
                _mm = t.matmul(pg[h][0:8, :], bf["gw_sb"][:, fc, :],
                         bf["h_sb"][:, fc * T + h * 512: fc * T + (h + 1) * 512],
                         start=(fc == 0), stop=(fc == NC - 1))
                _mm = t.matmul(plow[h][:], bf["A_sb"][:, fc, :],
                         bf["h_sb"][:, fc * T + h * 512: fc * T + (h + 1) * 512],
                         start=(fc == 0), stop=(fc == NC - 1))
        _mm.then_inc(*p.inc("t", 1))
    m_gate = p.cnt["t"]

    @blk.scalar
    def _(sc, m=m_gate):
        sc.wait_ge(p.sem["t"], m)
        sc.copy(bf["lgsb"][:, 0:512], pg[0][0:8, :])
        sc.copy(bf["lgsb"][:, 512:1024], pg[1][0:8, :]).then_inc(*p.inc("s", 1))
    m_lg = p.cnt["s"]

    @blk.tensor
    def _(t, m=m_lg):
        t.wait_ge(p.sem["s"], m)
        for s_ in range(8):
            _mm = t.transpose(bf["ptb"][:, s_ * 8:(s_ + 1) * 8], bf["lgsb"][:, s_ * 128:(s_ + 1) * 128],
                        bf["id_bf"][0:8, 0:8])
        _mm.then_inc(*p.inc("t", 1))
    m_pt = p.cnt["t"]

    def v3(x):
        return x[:].rearrange("p (s e) -> p s e", e=8)

    @blk.vector
    def _(v, m=m_pt):
        v.wait_ge(p.sem["t"], m)
        v.tensor_copy(bf["lgt"][:], bf["ptb"][:]).then_inc(*p.inc("v", 1))
        v.wait_ge(*p.at("v"))
        v.reduce_max(bf["m1"][:], v3(bf["lgt"]), axis=AX.X).then_inc(*p.inc("v", 1))
        v.wait_ge(*p.at("v"))
        m1b = bf["m1"][:].unsqueeze(2).broadcast_to([128, 8, 8])
        v.tensor_tensor(v3(bf["eq1"]), v3(bf["lgt"]), m1b, ALU.is_equal).then_inc(*p.inc("v", 1))
        v.wait_ge(*p.at("v"))
        v.scalar_tensor_tensor(v3(bf["msk"]), v3(bf["eq1"]), -1e9, v3(bf["lgt"]), ALU.mult, ALU.add).then_inc(*p.inc("v", 1))
        v.wait_ge(*p.at("v"))
        v.reduce_max(bf["m2"][:], v3(bf["msk"]), axis=AX.X).then_inc(*p.inc("v", 1))
        v.wait_ge(*p.at("v"))
        m2b = bf["m2"][:].unsqueeze(2).broadcast_to([128, 8, 8])
        v.tensor_tensor(v3(bf["eq2"]), v3(bf["msk"]), m2b, ALU.is_equal)
        v.tensor_sub(bf["d12"][:], bf["m1"][:], bf["m2"][:]).then_inc(*p.inc("v", 1))
    m_d12 = p.cnt["v"]

    @blk.scalar
    def _(sc, m=m_d12):
        sc.wait_ge(p.sem["v"], m)
        sc.activation(bf["w1"][:], bf["d12"][:], AF.Sigmoid).then_inc(*p.inc("s", 1))
    m_w1 = p.cnt["s"]

    @blk.vector
    def _(v, m=m_w1):
        v.wait_ge(p.sem["s"], m)
        v.tensor_scalar(bf["w2"][:], bf["w1"][:], -1.0, 1.0, ALU.mult, ALU.add).then_inc(*p.inc("v", 1))
        v.wait_ge(*p.at("v"))
        w1b = bf["w1"][:].unsqueeze(2).broadcast_to([128, 8, 8])
        w2b = bf["w2"][:].unsqueeze(2).broadcast_to([128, 8, 8])
        v.tensor_tensor(v3(bf["wf1"]), v3(bf["eq1"]), w1b, ALU.mult)
        v.tensor_tensor(v3(bf["wf2"]), v3(bf["eq2"]), w2b, ALU.mult).then_inc(*p.inc("v", 1))
        v.wait_ge(*p.at("v"))
        v.tensor_add(bf["wff"][:], bf["wf1"][:], bf["wf2"][:]).then_inc(*p.inc("v", 1))
    m_wff = p.cnt["v"]

    @blk.tensor
    def _(t, m=m_wff):
        t.wait_ge(p.sem["v"], m)
        for s_ in range(8):
            _mm = t.transpose(pt2[s_ // 4][0:8, (s_ % 4) * 128:(s_ % 4 + 1) * 128],
                        bf["wff"][:, s_ * 8:(s_ + 1) * 8], bf["id_f"][:])
        _mm.then_inc(*p.inc("t", 1))
    m_pt2 = p.cnt["t"]

    @blk.scalar
    def _(sc, m=m_pt2):
        sc.wait_ge(p.sem["t"], m)
        sc.copy(bf["wt_sb"][:, 0:512], pt2[0][0:8, :])
        sc.copy(bf["wt_sb"][:, 512:1024], pt2[1][0:8, :]).then_inc(*p.inc("s", 1))
    m_wt = p.cnt["s"]

    @blk.tensor
    def _(t, m=m_wt):
        t.wait_ge(p.sem["s"], m)
        t.wait_ge(p.sem["ld"], p.m_ld_consts)
        for h in range(2):
            _mm = t.matmul(pwexp[h][:], bf["E_sb"][:], bf["wt_sb"][0:8, h * 512:(h + 1) * 512], start=True, stop=True)
        _mm.then_inc(*p.inc("t", 1))
    m_wexp = p.cnt["t"]

    @blk.scalar
    def _(sc, m=m_wexp):
        sc.wait_ge(p.sem["t"], m)
        sc.activation(bf["wexp_sb"][:, 0:512], pwexp[0][:], AF.Copy, scale=2.0)
        sc.activation(bf["wexp_sb"][:, 512:1024], pwexp[1][:], AF.Copy, scale=2.0).then_inc(*p.inc("s", 1))
    m_wexp_sb = p.cnt["s"]

    @blk.vector
    def _(v, m=m_wexp_sb):
        v.wait_ge(p.sem["s"], m)
        v.tensor_tensor(bf["lw_bf"][:, 0:512], plow[0][:], bf["wexp_sb"][:, 0:512], ALU.mult)
        v.tensor_tensor(bf["lw_bf"][:, 512:1024], plow[1][:], bf["wexp_sb"][:, 512:1024], ALU.mult).then_inc(*p.inc("v", 1))
    m_lw = p.cnt["v"]

    m_mm, m_evac, m_add, m_fs, m_fold, m_st = {}, {}, {}, {}, {}, {}
    m_t_all = p.cnt["t"]
    for oc in range(NC):
        @blk.gpsimd
        def _(g, oc=oc, m_t_all=m_t_all):
            if oc > 0:
                g.wait_ge(p.sem["t"], m_mm[oc - 1])
            else:
                g.wait_ge(p.sem["t"], m_t_all)
            g.dma_start(out=bf["bw_dst"],
                        in_=c["bwT"].rearrange("(c p) o -> p c o", p=128)[:, :, oc * 128:(oc + 1) * 128]).then_inc(*p.inc("ldg", 16))
        m_bw = p.cnt["ldg"]

        @blk.tensor
        def _(t, oc=oc, m_bw=m_bw, m_lw=m_lw):
            t.wait_ge(p.sem["ldg"], m_bw)
            if oc == 0:
                t.wait_ge(p.sem["v"], m_lw)
            else:
                t.wait_ge(p.sem["s"], m_evac[oc - 1])
            for fc in range(NC):
                for h in range(2):
                    _mm = t.matmul(pb[h][:], bf["bw_fc"](fc),
                             bf["h_sb"][:, fc * T + h * 512: fc * T + (h + 1) * 512],
                             start=(fc == 0), stop=False)
            for h in range(2):
                _mm = t.matmul(pb[h][:], bf["Bc_sb"][:, oc, :], bf["lw_bf"][:, h * 512:(h + 1) * 512],
                         start=False, stop=False)
            for h in range(2):
                _mm = t.matmul(pb[h][:], bf["bias_bf"][0:1, oc * 128:(oc + 1) * 128], bf["ones512_bf"][:],
                         start=False, stop=True)
            _mm.then_inc(*p.inc("t", 1))
        m_mm[oc] = p.cnt["t"]

        @blk.sync
        def _(s, oc=oc):
            if oc > 0:
                s.wait_ge(p.sem["v"], m_add[oc - 1])
            s.dma_start(out=bf["xr"][:], in_=resid3[:, oc, :]).then_inc(*p.inc("ld", 16))
        m_xr = p.cnt["ld"]

        @blk.scalar
        def _(sc, oc=oc):
            sc.wait_ge(p.sem["t"], m_mm[oc])
            if oc > 0:
                sc.wait_ge(p.sem["v"], m_add[oc - 1])
            sc.activation(bf["tsb"][:, 0:512], pb[0][:], AF.Copy, scale=0.5)
            sc.activation(bf["tsb"][:, 512:1024], pb[1][:], AF.Copy, scale=0.5).then_inc(*p.inc("s", 1))
        m_evac[oc] = p.cnt["s"]

        @blk.vector
        def _(v, oc=oc, m_xr=m_xr):
            v.wait_ge(p.sem["s"], m_evac[oc])
            v.wait_ge(p.sem["ld"], m_xr)
            if oc > 0:
                v.wait_ge(p.sem["st"], m_st[oc - 1])
            v.tensor_add(bf["x1sb"][:], bf["xr"][:], bf["tsb"][:]).then_inc(*p.inc("v", 1))
        m_add[oc] = p.cnt["v"]

        if fold:
            @blk.scalar
            def _(sc, oc=oc):
                sc.wait_ge(p.sem["v"], m_add[oc])
                if oc > 0:
                    sc.wait_ge(p.sem["t"], m_fold[oc - 1])
                sc.copy(bf["xstat"][:, 0:T], bf["x1sb"][:])
                sc.activation(bf["xstat"][:, T:2 * T], bf["x1sb"][:], AF.Square).then_inc(*p.inc("s", 1))
            m_fs[oc] = p.cnt["s"]

            @blk.tensor
            def _(t, oc=oc):
                t.wait_ge(p.sem["s"], m_fs[oc])
                for h in range(2):
                    _mm = t.matmul(bk[2 + h][0:1, :], bf["ones128"][:], bf["xstat"][:, h * 512:(h + 1) * 512],
                             start=(oc == 0), stop=(oc == NC - 1))
                    _mm = t.matmul(bk[4 + h][0:1, :], bf["ones128"][:], bf["xstat"][:, T + h * 512:T + (h + 1) * 512],
                             start=(oc == 0), stop=(oc == NC - 1))
                _mm.then_inc(*p.inc("t", 1))
            m_fold[oc] = p.cnt["t"]

        @blk.scalar
        def _(sc, oc=oc):
            sc.wait_ge(p.sem["v"], m_add[oc])
            sc.dma_start(out=out3[:, oc, :], in_=bf["x1sb"][:]).then_inc(*p.inc("st", 16))
        m_st[oc] = p.cnt["st"]

    return m_st[NC - 1], (m_fold[NC - 1] if fold else None)


def emit_outproj(p, c, x1_3, out3):
    blk, bf = p.blk, p.buf
    bk = bf["bk"]
    pb = [bk[0], bk[1]]
    m_mm, m_evac, m_add, m_fs, m_fold, m_st = {}, {}, {}, {}, {}, {}
    for oc in range(NC):
        @blk.gpsimd
        def _(g, oc=oc):
            if oc > 0:
                g.wait_ge(p.sem["t"], m_mm[oc - 1])
            g.dma_start(out=bf["bw_dst"],
                        in_=c["owT"].rearrange("(c p) o -> p c o", p=128)[:, :, oc * 128:(oc + 1) * 128]).then_inc(*p.inc("ldg", 16))
        m_bw = p.cnt["ldg"]

        @blk.tensor
        def _(t, oc=oc, m_bw=m_bw):
            t.wait_ge(p.sem["ldg"], m_bw)
            if oc == 0:
                t.wait_ge(p.sem["ldg"], p.m_ctx_ldg)
                t.wait_ge(p.sem["v"], p.m_v_consts)
            else:
                t.wait_ge(p.sem["s"], m_evac[oc - 1])
            for fc in range(NC):
                for h in range(2):
                    _mm = t.matmul(pb[h][:], bf["bw_fc"](fc),
                             bf["h_sb"][:, fc * T + h * 512: fc * T + (h + 1) * 512],
                             start=(fc == 0), stop=False)
            for h in range(2):
                _mm = t.matmul(pb[h][:], bf["obias_bf"][0:1, oc * 128:(oc + 1) * 128], bf["ones512_bf"][:],
                         start=False, stop=True)
            _mm.then_inc(*p.inc("t", 1))
        m_mm[oc] = p.cnt["t"]

        @blk.sync
        def _(s, oc=oc):
            if oc > 0:
                s.wait_ge(p.sem["v"], m_add[oc - 1])
            s.dma_start(out=bf["xr"][:], in_=x1_3[:, oc, :]).then_inc(*p.inc("ld", 16))
        m_xr = p.cnt["ld"]

        @blk.scalar
        def _(sc, oc=oc):
            sc.wait_ge(p.sem["t"], m_mm[oc])
            if oc > 0:
                sc.wait_ge(p.sem["v"], m_add[oc - 1])
            sc.copy(bf["tsb"][:, 0:512], pb[0][:])
            sc.copy(bf["tsb"][:, 512:1024], pb[1][:]).then_inc(*p.inc("s", 1))
        m_evac[oc] = p.cnt["s"]

        @blk.vector
        def _(v, oc=oc, m_xr=m_xr):
            v.wait_ge(p.sem["s"], m_evac[oc])
            v.wait_ge(p.sem["ld"], m_xr)
            if oc > 0:
                v.wait_ge(p.sem["st"], m_st[oc - 1])
            v.tensor_add(bf["x1sb"][:], bf["xr"][:], bf["tsb"][:]).then_inc(*p.inc("v", 1))
        m_add[oc] = p.cnt["v"]

        @blk.scalar
        def _(sc, oc=oc):
            sc.wait_ge(p.sem["v"], m_add[oc])
            if oc > 0:
                sc.wait_ge(p.sem["t"], m_fold[oc - 1])
            sc.copy(bf["xstat"][:, 0:T], bf["x1sb"][:])
            sc.activation(bf["xstat"][:, T:2 * T], bf["x1sb"][:], AF.Square).then_inc(*p.inc("s", 1))
        m_fs[oc] = p.cnt["s"]

        @blk.tensor
        def _(t, oc=oc):
            t.wait_ge(p.sem["s"], m_fs[oc])
            for h in range(2):
                _mm = t.matmul(bk[2 + h][0:1, :], bf["ones128"][:], bf["xstat"][:, h * 512:(h + 1) * 512],
                         start=(oc == 0), stop=(oc == NC - 1))
                _mm = t.matmul(bk[4 + h][0:1, :], bf["ones128"][:], bf["xstat"][:, T + h * 512:T + (h + 1) * 512],
                         start=(oc == 0), stop=(oc == NC - 1))
            _mm.then_inc(*p.inc("t", 1))
        m_fold[oc] = p.cnt["t"]

        @blk.scalar
        def _(sc, oc=oc):
            sc.wait_ge(p.sem["v"], m_add[oc])
            sc.dma_start(out=out3[:, oc, :], in_=bf["x1sb"][:]).then_inc(*p.inc("st", 16))
        m_st[oc] = p.cnt["st"]
    return m_st[NC - 1], m_fold[NC - 1]


def emit_qkv(p, c, qkv3, m_h, w0_twait):
    blk, bf = p.blk, p.buf
    pq = bf["bk"][5]
    prev_t = {}
    prev_s = {}
    prev_v = {}
    prev_v2 = {}
    prev_st = {}
    first = True
    for oc in range(NOC_QKV):
        @blk.gpsimd
        def _(g, oc=oc):
            if oc > 0:
                g.wait_ge(p.sem["t"], prev_t[oc - 1])
            else:
                g.wait_ge(p.sem["t"], w0_twait)
            g.dma_start(out=bf["wq_sb"][:],
                        in_=c["qkvw"].rearrange("(c p) f -> p c f", p=128)[:, :, oc * 512:(oc + 1) * 512]).then_inc(*p.inc("ldg", 16))
            g.dma_start(out=bf["qb_row"][:], in_=c["qkvb"][0:1, oc * 512:(oc + 1) * 512]).then_inc(*p.inc("ldg", 16))
        m_w = p.cnt["ldg"]

        for tb in range(8):
            key = (oc, tb)
            pkey = (oc, tb - 1) if tb > 0 else (oc - 1, 7)

            @blk.tensor
            def _(t, oc=oc, tb=tb, m_w=m_w, pkey=pkey, first=first, m_h=m_h):
                t.wait_ge(p.sem["ldg"], m_w)
                if first:
                    t.wait_ge(p.sem["v"], m_h)
                else:
                    t.wait_ge(p.sem["s"], prev_s[pkey])
                _mm = t.matmul(pq[:], bf["ones_row_bf"][:], bf["qb_row"][:], start=True, stop=False)
                for fc in range(NC):
                    _mm = t.matmul(pq[:], bf["h_sb"][:, fc * T + tb * 128: fc * T + tb * 128 + 128],
                             bf["wq_sb"][:, fc, :], start=False, stop=(fc == NC - 1))
                _mm.then_inc(*p.inc("t", 1))
            prev_t[oc] = p.cnt["t"]
            m_qm = p.cnt["t"]

            if oc < 16:
                @blk.scalar
                def _(sc, m_qm=m_qm, first=first, pkey=pkey):
                    sc.wait_ge(p.sem["t"], m_qm)
                    if not first and pkey in prev_v:
                        sc.wait_ge(p.sem["v"], prev_v[pkey])
                    sc.copy(bf["qsb"][:], pq[:]).then_inc(*p.inc("s", 1))
                prev_s[key] = p.cnt["s"]
                m_qe = p.cnt["s"]

                @blk.vector
                def _(v, oc=oc, tb=tb, m_qe=m_qe, first=first, pkey=pkey):
                    v.wait_ge(p.sem["s"], m_qe)
                    cos_rep = bf["cos_sb"][:, tb, :].unsqueeze(1).broadcast_to([128, 4, 128])
                    sin_rep = bf["sin_sb"][:, tb, :].rearrange("p (h c) -> p h c", h=2).unsqueeze(1).broadcast_to([128, 4, 2, 64])
                    qv = bf["qsb"][:].rearrange("p (r c) -> p r c", r=4)
                    qsw = bf["qsb"][:].rearrange("p (r h c) -> p r h c", r=4, h=2)[:, :, ::-1, :]
                    v.tensor_tensor(bf["rop1"][:].rearrange("p (r c) -> p r c", r=4), qv, cos_rep, ALU.mult)
                    v.tensor_tensor(bf["rop2"][:].rearrange("p (r h c) -> p r h c", r=4, h=2), qsw, sin_rep, ALU.mult).then_inc(*p.inc("v", 1))
                    if not first and pkey in prev_st:
                        v.wait_ge(p.sem["st"], prev_st[pkey])
                    v.wait_ge(*p.at("v"))
                    v.tensor_add(bf["ropo"][:], bf["rop1"][:], bf["rop2"][:]).then_inc(*p.inc("v", 1))
                prev_v[key] = p.cnt["v"] - 1
                prev_v2[key] = p.cnt["v"]
                m_ro = p.cnt["v"]

                @blk.scalar
                def _(sc, oc=oc, tb=tb, m_ro=m_ro):
                    sc.wait_ge(p.sem["v"], m_ro)
                    sc.dma_start(out=qkv3[:, tb, oc * 512:(oc + 1) * 512], in_=bf["ropo"][:]).then_inc(*p.inc("st", 16))
                prev_st[key] = p.cnt["st"]
            else:
                @blk.scalar
                def _(sc, m_qm=m_qm, first=first, pkey=pkey):
                    sc.wait_ge(p.sem["t"], m_qm)
                    if not first:
                        if pkey in prev_st:
                            sc.wait_ge(p.sem["st"], prev_st[pkey])
                        if pkey in prev_v2:
                            sc.wait_ge(p.sem["v"], prev_v2[pkey])
                    sc.activation(bf["ropo"][:], pq[:], AF.Copy).then_inc(*p.inc("s", 1))
                prev_s[key] = p.cnt["s"]
                m_ve = p.cnt["s"]

                @blk.scalar
                def _(sc, oc=oc, tb=tb, m_ve=m_ve):
                    sc.wait_ge(p.sem["s"], m_ve)
                    sc.dma_start(out=qkv3[:, tb, oc * 512:(oc + 1) * 512], in_=bf["ropo"][:]).then_inc(*p.inc("st", 16))
                prev_st[key] = p.cnt["st"]
            first = False


def build_l1():
    nc = bass.Bass("TRN2", target_bir_lowering=False, debug=False, num_devices=8)
    c = {}
    c["xT"] = nc.dram_tensor("xT", [D, T], F32, kind="ExternalInput").ap()
    c["gwT"] = nc.dram_tensor("gwT", [D, 8], F32, kind="ExternalInput").ap()
    c["AT"] = nc.dram_tensor("AT", [D, 128], F32, kind="ExternalInput").ap()
    c["BcT"] = nc.dram_tensor("BcT", [128, D], F32, kind="ExternalInput").ap()
    c["bwT"] = nc.dram_tensor("bwT", [D, D], F32, kind="ExternalInput").ap()
    c["bb"] = nc.dram_tensor("bb", [1, D], F32, kind="ExternalInput").ap()
    c["qkvw"] = nc.dram_tensor("qkvw", [D, QKVF], F32, kind="ExternalInput").ap()
    c["qkvb"] = nc.dram_tensor("qkvb", [1, QKVF], F32, kind="ExternalInput").ap()
    c["cosT"] = nc.dram_tensor("cosT", [T, 128], F32, kind="ExternalInput").ap()
    c["sinT"] = nc.dram_tensor("sinT", [T, 128], F32, kind="ExternalInput").ap()
    c["idm"] = nc.dram_tensor("idm", [128, 128], F32, kind="ExternalInput").ap()
    c["Emat"] = nc.dram_tensor("Emat", [8, 128], F32, kind="ExternalInput").ap()
    x1T = nc.dram_tensor("x1T", [D, T], F32, kind="ExternalOutput").ap()
    qkvtok = nc.dram_tensor("qkvtok", [T, QKVF], F32, kind="ExternalOutput").ap()

    import contextlib
    es = contextlib.ExitStack()
    blk = es.enter_context(nc.Block())
    p = P(nc, blk, es)
    _alloc_common(p, es, with_qkv=True)
    emit_consts(p, c, with_qkv=True, with_obias=False)
    bk = p.buf["bk"]
    src3 = c["xT"].rearrange("(c p) t -> p c t", p=128)
    x1_3 = x1T.rearrange("(c p) t -> p c t", p=128)
    qkv3 = qkvtok.rearrange("(n p) f -> p n f", p=128)

    m_stats = emit_ln_stats_stream(p, src3, pm=[bk[0], bk[1]], psq=[bk[2], bk[3]])
    m_bcast = emit_ln_scales(p, [bk[0], bk[1]], [bk[2], bk[3]], prs=[bk[4], bk[5]], pc=[bk[6], bk[0]], m_stats_t=m_stats)
    m_h = emit_ln_norm(p, src3, [bk[4], bk[5]], [bk[6], bk[0]], m_bcast)
    m_st1, m_fold = emit_ffn(p, c, src3, x1_3, m_h, fold=True)
    m_bcast2 = emit_ln_scales(p, [bk[2], bk[3]], [bk[4], bk[5]], prs=[bk[6], bk[0]], pc=[bk[1], bk[2]], m_stats_t=m_fold)
    m_h2 = emit_ln_norm(p, x1_3, [bk[6], bk[0]], [bk[1], bk[2]], m_bcast2, extra_first_waits=(("st", m_st1),))
    emit_qkv(p, c, qkv3, m_h2, w0_twait=m_fold)

    @blk.scalar
    def _(sc):
        sc.wait_ge(*p.at("st"))
    es.close()
    return nc


def build_l2():
    nc = bass.Bass("TRN2", target_bir_lowering=False, debug=False, num_devices=8)
    qf = nc.dram_tensor("qf", [8 * 128, S], F32, kind="ExternalInput").ap()
    kf = nc.dram_tensor("kf", [8 * 128, S], F32, kind="ExternalInput").ap()
    vt = nc.dram_tensor("vt", [8 * S, 128], F32, kind="ExternalInput").ap()
    msks = nc.dram_tensor("msks", [4 * 128, 512], F32, kind="ExternalInput").ap()
    ctxo = nc.dram_tensor("ctxo", [8 * 128, S], F32, kind="ExternalOutput").ap()

    import contextlib
    es = contextlib.ExitStack()
    blk = es.enter_context(nc.Block())
    p = P(nc, blk, es)
    bf = p.buf
    sb = lambda n, shp, dt: es.enter_context(nc.sbuf_tensor(n, shp, dt))
    ps = lambda n, shp, dt: es.enter_context(nc.psum_tensor(n, shp, dt))
    bf["q_sb"] = sb("q_sb", [128, S], BF16)
    bf["k_sb"] = sb("k_sb", [128, S], BF16)
    bf["v_sb"] = sb("v_sb", [128, 32, 128], BF16)
    bf["mask_sb"] = sb("mask_sb", [128, 4 * 512], BF16)
    bf["pexp"] = sb("pexp", [128, 512], BF16)
    bf["ctxf"] = sb("ctxf", [128, 512], F32)
    bf["ctxos"] = sb("ctxos", [128, 512], F32)
    bf["rrow"] = sb("rrow", [1, 512], F32)
    bf["ones1"] = sb("ones1", [128, 1], BF16)
    bf["ones_row_f"] = sb("ones_row_f", [1, 128], F32)
    bf["shiftc"] = sb("shiftc", [128, 1], F32)
    ps_s = ps("ps_s", [128, 512], F32)
    ps_c = ps("ps_c", [128, 512], F32)
    ps_r = ps("ps_r", [1, 512], F32)
    ps_b = ps("ps_b", [128, 512], F32)

    @blk.vector
    def _(v):
        v.memset(bf["ones1"][:], 1.0)
        v.memset(bf["ones_row_f"][:], 1.0)
        v.memset(bf["shiftc"][:], SHIFT).then_inc(*p.inc("v", 1))
    m_ones = p.cnt["v"]

    @blk.gpsimd
    def _(g):
        g.dma_start(out=bf["mask_sb"][:].rearrange("p (m q) -> p m q", m=4),
                    in_=msks.rearrange("(m p) q -> p m q", p=128)).then_inc(*p.inc("ldg", 16))
    m_mask = p.cnt["ldg"]

    m_unit_t = None
    for u in range(8):
        @blk.gpsimd
        def _(g, u=u, m_unit_t=m_unit_t):
            if m_unit_t is not None:
                g.wait_ge(p.sem["t"], m_unit_t)
            g.dma_start(out=bf["q_sb"][:], in_=qf[u * 128:(u + 1) * 128, :]).then_inc(*p.inc("ldg", 16))
            g.dma_start(out=bf["k_sb"][:], in_=kf[u * 128:(u + 1) * 128, :]).then_inc(*p.inc("ldg", 16))
            g.dma_start(out=bf["v_sb"][:],
                        in_=vt[u * S:(u + 1) * S, :].rearrange("(n p) d -> p n d", p=128)).then_inc(*p.inc("ldg", 16))
        m_qkv = p.cnt["ldg"]

        for qb in range(8):
            nkt = 4 * qb + 4
            m_exp_prev = None
            m_maskv_prev = None
            for kt in range(nkt):
                @blk.tensor
                def _(t, u=u, qb=qb, kt=kt, m_qkv=m_qkv, m_exp_prev=m_exp_prev, m_maskv_prev=m_maskv_prev):
                    if kt == 0:
                        t.wait_ge(p.sem["ldg"], m_qkv)
                        t.wait_ge(p.sem["v"], p.m_l2_vfree if hasattr(p, "m_l2_vfree") else m_ones)
                        t.wait_ge(p.sem["s"], p.m_l2_sfree if hasattr(p, "m_l2_sfree") else 0)
                    else:
                        if m_maskv_prev is not None:
                            t.wait_ge(p.sem["v"], m_maskv_prev)
                        else:
                            t.wait_ge(p.sem["s"], m_exp_prev)
                    _mm = t.matmul(ps_s[:], bf["k_sb"][:, kt * 128:(kt + 1) * 128],
                             bf["q_sb"][:, qb * 512:(qb + 1) * 512], start=True, stop=True)
                    _mm.then_inc(*p.inc("t", 1))
                m_sc = p.cnt["t"]

                @blk.scalar
                def _(sc, m_sc=m_sc):
                    sc.wait_ge(p.sem["t"], m_sc)
                    sc.activation(bf["pexp"][:], ps_s[:], AF.Exp, scale=INV_SQRT_D,
                                  bias=bf["shiftc"][:, 0:1]).then_inc(*p.inc("s", 1))
                m_exp = p.cnt["s"]
                m_exp_prev = m_exp

                pat = kt - 4 * qb
                if pat >= 0:
                    @blk.vector
                    def _(v, pat=pat, m_exp=m_exp, m_mask=m_mask):
                        v.wait_ge(p.sem["s"], m_exp)
                        v.wait_ge(p.sem["ldg"], m_mask)
                        v.tensor_tensor(bf["pexp"][:], bf["pexp"][:],
                                        bf["mask_sb"][:, pat * 512:(pat + 1) * 512], ALU.mult).then_inc(*p.inc("v", 1))
                    m_maskv_prev = p.cnt["v"]
                    m_pready = ("v", m_maskv_prev)
                else:
                    m_maskv_prev = None
                    m_pready = ("s", m_exp)

                @blk.tensor
                def _(t, u=u, qb=qb, kt=kt, nkt=nkt, m_pready=m_pready):
                    t.wait_ge(p.sem[m_pready[0]], m_pready[1])
                    _mm = t.matmul(ps_c[:], bf["v_sb"][:, kt, :], bf["pexp"][:], start=(kt == 0), stop=(kt == nkt - 1))
                    _mm = t.matmul(ps_r[0:1, :], bf["ones1"][:], bf["pexp"][:], start=(kt == 0), stop=(kt == nkt - 1))
                    _mm.then_inc(*p.inc("t", 1))
            m_ctx = p.cnt["t"]

            @blk.vector
            def _(v, m_ctx=m_ctx):
                v.wait_ge(p.sem["t"], m_ctx)
                v.reciprocal(bf["rrow"][:], ps_r[0:1, :]).then_inc(*p.inc("v", 1))
            m_rec = p.cnt["v"]

            @blk.tensor
            def _(t, m_rec=m_rec):
                t.wait_ge(p.sem["v"], m_rec)
                _mm = t.matmul(ps_b[:], bf["ones_row_f"][:], bf["rrow"][:], start=True, stop=True)
                _mm.then_inc(*p.inc("t", 1))
            m_bc = p.cnt["t"]

            @blk.scalar
            def _(sc, m_bc=m_bc):
                sc.wait_ge(p.sem["t"], m_bc)
                sc.copy(bf["ctxf"][:], ps_c[:]).then_inc(*p.inc("s", 1))
            m_cf = p.cnt["s"]

            @blk.vector
            def _(v, m_cf=m_cf):
                v.wait_ge(p.sem["s"], m_cf)
                if p.cnt["st"] > 0:
                    v.wait_ge(*p.at("st"))
                v.tensor_tensor(bf["ctxos"][:], bf["ctxf"][:], ps_b[:], ALU.mult).then_inc(*p.inc("v", 1))
            m_cm = p.cnt["v"]

            @blk.scalar
            def _(sc, u=u, qb=qb, m_cm=m_cm):
                sc.wait_ge(p.sem["v"], m_cm)
                sc.dma_start(out=ctxo[u * 128:(u + 1) * 128, qb * 512:(qb + 1) * 512],
                             in_=bf["ctxos"][:]).then_inc(*p.inc("st", 16))
            p.m_l2_sfree = p.cnt["s"]
            p.m_l2_vfree = p.cnt["v"]
        m_unit_t = p.cnt["t"]

    @blk.scalar
    def _(sc):
        sc.wait_ge(*p.at("st"))
    es.close()
    return nc


def build_l3():
    nc = bass.Bass("TRN2", target_bir_lowering=False, debug=False, num_devices=8)
    c = {}
    c["ctxT"] = nc.dram_tensor("ctxT", [D, T], F32, kind="ExternalInput").ap()
    c["x1T"] = nc.dram_tensor("x1T", [D, T], F32, kind="ExternalInput").ap()
    c["owT"] = nc.dram_tensor("owT", [D, D], F32, kind="ExternalInput").ap()
    c["ob"] = nc.dram_tensor("ob", [1, D], F32, kind="ExternalInput").ap()
    c["gwT"] = nc.dram_tensor("gwT", [D, 8], F32, kind="ExternalInput").ap()
    c["AT"] = nc.dram_tensor("AT", [D, 128], F32, kind="ExternalInput").ap()
    c["BcT"] = nc.dram_tensor("BcT", [128, D], F32, kind="ExternalInput").ap()
    c["bwT"] = nc.dram_tensor("bwT", [D, D], F32, kind="ExternalInput").ap()
    c["bb"] = nc.dram_tensor("bb", [1, D], F32, kind="ExternalInput").ap()
    c["idm"] = nc.dram_tensor("idm", [128, 128], F32, kind="ExternalInput").ap()
    c["Emat"] = nc.dram_tensor("Emat", [8, 128], F32, kind="ExternalInput").ap()
    outT = nc.dram_tensor("outT", [D, T], F32, kind="ExternalOutput").ap()
    x2T = nc.dram_tensor("x2T", [D, T], F32, kind="Internal").ap()

    import contextlib
    es = contextlib.ExitStack()
    blk = es.enter_context(nc.Block())
    p = P(nc, blk, es)
    _alloc_common(p, es, with_qkv=False)
    p.buf["obias_bf"] = es.enter_context(nc.sbuf_tensor("obias_bf", [1, D], BF16))
    emit_consts(p, c, with_qkv=False, with_obias=True)
    bk = p.buf["bk"]

    @blk.gpsimd
    def _(g):
        g.dma_start(out=p.buf["h_sb"][:].rearrange("p (c t) -> p c t", t=T),
                    in_=c["ctxT"].rearrange("(c p) t -> p c t", p=128)).then_inc(*p.inc("ldg", 16))
    p.m_ctx_ldg = p.cnt["ldg"]

    x1_3 = c["x1T"].rearrange("(c p) t -> p c t", p=128)
    x2_3 = x2T.rearrange("(c p) t -> p c t", p=128)
    out3 = outT.rearrange("(c p) t -> p c t", p=128)

    m_st2, m_fold3 = emit_outproj(p, c, x1_3, x2_3)
    m_bcast3 = emit_ln_scales(p, [bk[2], bk[3]], [bk[4], bk[5]], prs=[bk[6], bk[0]], pc=[bk[1], bk[2]], m_stats_t=m_fold3)
    m_h3 = emit_ln_norm(p, x2_3, [bk[6], bk[0]], [bk[1], bk[2]], m_bcast3, extra_first_waits=(("st", m_st2),))
    emit_ffn(p, c, x2_3, out3, m_h3, fold=False)

    @blk.scalar
    def _(sc):
        sc.wait_ge(*p.at("st"))
    es.close()
    return nc


_PROGS = {}


def _get_progs():
    if "l1" not in _PROGS:
        _PROGS["l1"] = build_l1()
        _PROGS["l2"] = build_l2()
        _PROGS["l3"] = build_l3()
    return _PROGS


def kernel(L0_attn_hidden_states, L0_attn_qkv_w, L0_attn_qkv_b, L0_attn_out_w,
           L0_attn_out_b, L0_ffn_gate_w, L0_ffn_base_w, L0_ffn_base_b,
           L0_ffn_lora_A, L0_ffn_lora_B):
    progs = _get_progs()
    f32 = np.float32
    x = np.asarray(L0_attn_hidden_states, dtype=f32).reshape(2 * S, D)
    gwT = np.ascontiguousarray(np.asarray(L0_ffn_gate_w, f32).T)
    bwT = np.ascontiguousarray(np.asarray(L0_ffn_base_w, f32).T)
    AT = np.ascontiguousarray(np.asarray(L0_ffn_lora_A, f32).reshape(128, D).T)
    BcT = np.ascontiguousarray(np.asarray(L0_ffn_lora_B, f32).transpose(0, 2, 1).reshape(128, D))
    bb = np.ascontiguousarray(np.asarray(L0_ffn_base_b, f32).reshape(1, D))
    qkvw = np.ascontiguousarray(np.asarray(L0_attn_qkv_w, f32))
    qkvb = np.ascontiguousarray(np.asarray(L0_attn_qkv_b, f32).reshape(1, QKVF))
    owT = np.ascontiguousarray(np.asarray(L0_attn_out_w, f32))
    ob = np.ascontiguousarray(np.asarray(L0_attn_out_b, f32).reshape(1, D))
    idm = np.eye(128, dtype=f32)
    Emat = np.zeros((8, 128), dtype=f32)
    for e in range(8):
        Emat[e, 16 * e:16 * e + 16] = 1.0

    inv_freq = 1.0 / (10000.0 ** (np.arange(0, 64, dtype=np.float64) / 64.0))
    ins1 = []
    for cid in range(NCORES):
        pos = (np.arange(T, dtype=np.int64) + cid * T) % S
        ang = pos[:, None].astype(np.float64) * inv_freq[None, :]
        cosv = np.cos(ang)
        sinv = np.sin(ang)
        cosT = np.concatenate([cosv, cosv], axis=1).astype(f32)
        sinT = np.concatenate([-sinv, sinv], axis=1).astype(f32)
        xT = np.ascontiguousarray(x[cid * T:(cid + 1) * T, :].T)
        ins1.append(dict(xT=xT, gwT=gwT, AT=AT, BcT=BcT, bwT=bwT, bb=bb,
                         qkvw=qkvw, qkvb=qkvb, cosT=cosT, sinT=sinT, idm=idm, Emat=Emat))
    r1 = run_bass_kernel_spmd(progs["l1"], ins1, core_ids=list(range(NCORES)))
    x1T_full = np.concatenate([np.asarray(r1.results[cid]["x1T"]) for cid in range(NCORES)], axis=1)
    qkv_full = np.concatenate([np.asarray(r1.results[cid]["qkvtok"]) for cid in range(NCORES)], axis=0)

    kk = np.arange(128)[:, None]
    qq = np.arange(512)[None, :]
    msks = np.concatenate([(qq >= kk + 128 * pat).astype(f32) for pat in range(4)], axis=0)

    qg = qkv_full[:, 0:D]
    kg = qkv_full[:, D:2 * D]
    vg = qkv_full[:, 2 * D:3 * D]
    ins2 = []
    for cid in range(NCORES):
        units = [(b, h) for b in range(2) for h in range(4 * cid, 4 * cid + 4)]
        qf = np.concatenate([np.ascontiguousarray(qg[S * b:S * (b + 1), 128 * h:128 * (h + 1)].T) for (b, h) in units], axis=0)
        kf = np.concatenate([np.ascontiguousarray(kg[S * b:S * (b + 1), 128 * h:128 * (h + 1)].T) for (b, h) in units], axis=0)
        vtc = np.concatenate([np.ascontiguousarray(vg[S * b:S * (b + 1), 128 * h:128 * (h + 1)]) for (b, h) in units], axis=0)
        ins2.append(dict(qf=qf, kf=kf, vt=vtc, msks=msks))
    r2 = run_bass_kernel_spmd(progs["l2"], ins2, core_ids=list(range(NCORES)))

    ctx_full = np.empty((2 * S, D), dtype=f32)
    for cid in range(NCORES):
        co = np.asarray(r2.results[cid]["ctxo"])
        units = [(b, h) for b in range(2) for h in range(4 * cid, 4 * cid + 4)]
        for j, (b, h) in enumerate(units):
            ctx_full[S * b:S * (b + 1), 128 * h:128 * (h + 1)] = co[j * 128:(j + 1) * 128, :].T

    ins3 = []
    for cid in range(NCORES):
        ctxT = np.ascontiguousarray(ctx_full[cid * T:(cid + 1) * T, :].T)
        x1Tc = np.ascontiguousarray(x1T_full[:, cid * T:(cid + 1) * T])
        ins3.append(dict(ctxT=ctxT, x1T=x1Tc, owT=owT, ob=ob, gwT=gwT, AT=AT,
                         BcT=BcT, bwT=bwT, bb=bb, idm=idm, Emat=Emat))
    r3 = run_bass_kernel_spmd(progs["l3"], ins3, core_ids=list(range(NCORES)))
    outT_full = np.concatenate([np.asarray(r3.results[cid]["outT"]) for cid in range(NCORES)], axis=1)
    return np.ascontiguousarray(outT_full.T).reshape(2, S, D).astype(np.float32)
